# version 31 (final)
# baseline (speedup 1.0000x reference)
import numpy as np
import ml_dtypes
import concourse.bass as bass
import concourse.mybir as mybir
from concourse.bass_utils import run_bass_kernel_spmd
from contextlib import ExitStack

F32 = mybir.dt.float32
BF16 = mybir.dt.bfloat16
AF = mybir.ActivationFunctionType
OP = mybir.AluOpType
AX = mybir.AxisListType

E = 1024
H = 16
HD = 64
NE = 16
NX = NE + 1
INTER = 512
TOPK = 4
ROUTE_SCALE = 2.5
LN_EPS = 1e-5
BIG = 1.0e30

ENGS = ("sync", "gpsimd", "tensor", "vector", "scalar")


class Plan:
    def __init__(self):
        self.q = {e: [] for e in ENGS}
        self.cnt = {}
        self.floor = {}

    def sem(self, name):
        self.cnt[name] = 0
        return name

    def wait(self, eng, sem, val):
        if val <= 0:
            return
        key = (eng, sem)
        if self.floor.get(key, 0) >= val:
            return
        self.floor[key] = val
        self.q[eng].append(("w", sem, val))

    def setfloor(self, eng, sem, val):
        key = (eng, sem)
        if self.floor.get(key, 0) < val:
            self.floor[key] = val

    def op(self, eng, fn, inc=None, n=1, lp=None):
        self.q[eng].append(("o", fn, inc, n, lp))
        if inc is not None:
            self.cnt[inc] += n
            return self.cnt[inc]
        return None


def build(S=2048, debug=(), stop_after=None):
    T = S // 128
    TG = S // 512
    NQ = S // 4
    NJ = max(1, NQ // 512)
    assert NJ >= 1 and S % 512 == 0
    P4 = T // 4

    nc = bass.Bass("TRN2", target_bir_lowering=False)

    x = nc.dram_tensor("x", [S, E], F32, kind="ExternalInput")
    wqkv = nc.dram_tensor("wqkv", [E, 3 * E], F32, kind="ExternalInput")
    bqk = nc.dram_tensor("bqk", [128, 16], F32, kind="ExternalInput")
    bv = nc.dram_tensor("bv", [1, E], F32, kind="ExternalInput")
    wout = nc.dram_tensor("wout", [E, E], F32, kind="ExternalInput")
    bout = nc.dram_tensor("bout", [1, E], F32, kind="ExternalInput")
    gw = nc.dram_tensor("gw", [E, NE], F32, kind="ExternalInput")
    gbb = nc.dram_tensor("gbb", [128, NE], F32, kind="ExternalInput")
    w13 = nc.dram_tensor("w13", [NX, E, 2 * INTER], F32, kind="ExternalInput")
    w2 = nc.dram_tensor("w2", [NX, INTER, E], F32, kind="ExternalInput")
    ident = nc.dram_tensor("ident", [128, 128], BF16, kind="ExternalInput")
    y = nc.dram_tensor("y", [S, E], F32, kind="ExternalOutput")

    dbg_tensors = {}

    X0SEMS = ("s_x0a", "s_x0b", "s_x0c", "s_x0d", "s_x0e", "s_x0f", "s_x0g", "s_x0h")
    p = Plan()
    for s in ("s_cst", "g_cst", "g_x0", "g_x1", "g_xf", "g_wq0", "g_wq1", "g_wo", "g_rb",
              "g_w13a", "g_w13b", "g_w2a", "g_w2b",
              "s_x0a", "s_x0b", "s_x0c", "s_x0d",
              "s_x0e", "s_x0f", "s_x0g", "s_x0h", "s_y0", "s_y1", "s_dbg"):
        p.sem(s)
    for s in ("g_ms", "t_tp", "v_tp", "t_qk", "a_qk", "t_v", "v_v",
              "t_s", "a_e", "t_av", "v_rc", "v_os", "v_div",
              "t_y", "v_res", "v_ln", "a_sq", "v_ap", "a_xc",
              "t_tp2", "v_tp2", "t_g", "a_g", "v_tk",
              "t_h13", "a_si", "v_h", "t_w2", "v_y",
              "v_ln2", "a_sq2", "v_o"):
        p.sem(s)

    with ExitStack() as ctx:
        EC = ctx.enter_context

        SZ_XF = 8 * S * 2
        SZ_QF = 8 * S * 2
        SZ_KF = 8 * S * 2
        SZ_VT = max(T * H * 65 * 2, 32768)
        SZ_AF = 8 * S * 2
        SZ_WQS = 8 * 1024 * 2
        off = {}
        o = 0
        for name, sz in (("XF", SZ_XF), ("QF", SZ_QF), ("KF", SZ_KF),
                         ("VT", SZ_VT), ("attnF", SZ_AF),
                         ("wqs0", SZ_WQS), ("wqs1", SZ_WQS)):
            off[name] = o
            o += (sz + 31) // 32 * 32
        misc = [("ident_sb", 128, 128, BF16), ("ones", 1, 128, BF16),
                ("bqk_sb", 128, 16, F32), ("gbb_sb", 128, NE, F32),
                ("gw_sb", 128, 8 * NE, BF16), ("bv_sb", 1, E, BF16),
                ("bout_sb", 1, E, BF16),
                ("scoresg", 128, NE * T, F32), ("combT", 128, NE * T, F32),
                ("stats", 128, 12, F32), ("mv", 128, 2 * T, F32),
                ("sq", 128, T, F32), ("rs", 128, T, F32),
                ("tkW", 128, NE * T, F32),
                ("tkM", 128, NE * T, F32), ("tkm", 128, T, F32),

                ("mv2", 128, 2 * T, F32), ("sq2", 128, T, F32), ("rs2", 128, T, F32),
                ("eps", 128, 1, F32), ("nm", 128, 16, F32)]
        for name, pd, fd, dt in misc:
            sz = fd * (4 if dt == F32 else 2)
            off[name] = o
            o += (sz + 31) // 32 * 32
        total = o

        rng_ = nc.bump_sbuf(total)
        assert rng_ is not None, f"SBUF over budget: need {total} bytes/partition"
        base0, _end0 = rng_

        def at(name, shape, dt, rel):
            return nc.alloc_sbuf_tensor_at(name, shape, dt, offset=base0 + rel)

        XF = at("XF", [128, 8 * S], BF16, off["XF"])
        QF = at("QF", [128, 8 * S], BF16, off["QF"])
        KF = at("KF", [128, 8 * S], BF16, off["KF"])
        VT = at("VT", [128, T * H * 65], BF16, off["VT"])
        attnF = at("attnF", [128, 8 * S], BF16, off["attnF"])
        wqs = [at("wqs0", [128, 8192], BF16, off["wqs0"]),
               at("wqs1", [128, 8192], BF16, off["wqs1"])]
        sb = {}
        for name, pd, fd, dt in misc:
            sb[name] = at(name, [pd, fd], dt, off[name])
        RES = at("RES", [128, T * 1024], F32, off["QF"])
        woutsb = at("woutsb", [128, 8192], BF16, off["XF"])
        w13sb = [at("w13a", [128, 8192], BF16, off["VT"]),
                 at("w13b", [128, 8192], BF16, off["VT"] + 16384)]
        hsb = [at("ha", [128, 4 * S], BF16, off["attnF"]),
               at("hb", [128, 4 * S], BF16, off["attnF"] + 4 * S * 2)]
        probsP = [at("probsp0", [128, 2 * NQ], BF16, off["wqs0"]),
                  at("probsp1", [128, 2 * NQ], BF16, off["wqs0"] + 2 * NQ * 2)]
        NSL = (2 * S * 2 - 4 * NQ * 2) // 1024
        sb["silu_sb"] = at("silu2", [128, NSL * 512], BF16, off["wqs0"] + 4 * NQ * 2)
        w2sb = [at("w2a", [128, 4096], BF16, off["wqs0"] + 2 * S * 2),
                at("w2b", [128, 4096], BF16, off["wqs1"])]
        xrB = at("xrB", [128, 4096], BF16, off["wqs1"] + 8192)
        outr = at("outr", [128, 2048], F32, off["wqs1"] + 8192)
        if 16 * S >= 16384 + 8 * S:
            scr = off["XF"] + 16384
        else:
            scr, _ = nc.bump_sbuf(8 * S)
            scr -= base0
        sb["Ost"] = at("Ost", [128, S], BF16, scr)
        sb["rb"] = at("rb2", [128, S], BF16, scr + 2 * S)
        sb["sums"] = at("sums", [1, 2 * S], BF16, scr + 4 * S)
        xrA = at("xrA", [128, 8192], BF16, off["QF"])

        nc.psum_base = 6
        with nc.psum_tensor("tp0", [128, 1024], BF16) as tp0, \
             nc.psum_tensor("tp1", [128, 1024], BF16) as tp1:
            tp = [tp0, tp1]
        nc.psum_base = 0
        with nc.psum_tensor("qk0", [128, S], F32) as qk0, \
             nc.psum_tensor("qk1", [128, S], F32) as qk1:
            qk = [qk0, qk1]
        with nc.psum_tensor("vp0", [128, 1024], F32) as vp0, \
             nc.psum_tensor("vp1", [128, 1024], F32) as vp1:
            vp = [vp0, vp1]
        with nc.psum_tensor("SQ0", [128, 2 * NQ], F32) as SQ0, \
             nc.psum_tensor("SQ1", [128, 2 * NQ], F32) as SQ1, \
             nc.psum_tensor("OA0", [65, NQ], F32) as OA0, \
             nc.psum_tensor("OB0", [65, NQ], F32) as OB0, \
             nc.psum_tensor("OA1", [65, NQ], F32) as OA1, \
             nc.psum_tensor("OB1", [65, NQ], F32) as OB1:
            Sq = [SQ0, SQ1]
            O2 = [[OA0, OB0], [OA1, OB1]]
        with nc.psum_tensor("yp0", [128, 1024], F32) as yp0, \
             nc.psum_tensor("yp1", [128, 1024], F32) as yp1, \
             nc.psum_tensor("tq0", [128, 1024], BF16) as tq0, \
             nc.psum_tensor("tq1", [128, 1024], BF16) as tq1, \
             nc.psum_tensor("gpq", [128, 16 * NE], F32) as gpq:
            yp = [yp0, yp1]
            tq = [tq0, tq1]
        with nc.psum_tensor("hg0", [128, 512], F32) as hg0, \
             nc.psum_tensor("hu0", [128, 512], F32) as hu0, \
             nc.psum_tensor("hg1", [128, 512], F32) as hg1, \
             nc.psum_tensor("hu1", [128, 512], F32) as hu1, \
             nc.psum_tensor("y20", [128, 1024], F32) as y20, \
             nc.psum_tensor("y21", [128, 1024], F32) as y21:
            hg = [hg0, hg1]
            hu = [hu0, hu1]
            y2 = [y20, y21]

        sems = {}
        for name in p.cnt:
            sems[name] = EC(nc.semaphore(name))


        p.op("gpsimd", lambda g: g.memset(sb["ones"][:], 1.0), inc="g_ms")
        p.op("gpsimd", lambda g: g.memset(sb["eps"][:], LN_EPS), inc="g_ms")
        p.op("gpsimd", lambda g: g.memset(
            bass.AP(VT, 64, [[T * H * 65, 128], [65, T * H]]), 1.0), inc="g_ms")
        p.op("gpsimd", lambda g: g.dma_start(sb["bv_sb"][:], bv[:, :]), inc="g_cst", n=16)
        p.op("gpsimd", lambda g: g.dma_start(sb["bout_sb"][:], bout[:, :]), inc="g_cst", n=16)
        p.op("gpsimd", lambda g: g.dma_start(
            sb["gw_sb"][:], bass.AP(gw, 0, [[NE, 128], [128 * NE, 8], [1, NE]])),
            inc="g_cst", n=16)

        xpieces = [(0, 1, "g_xf"), (1, P4 - 1, "g_x0")] + [
            (pc * P4, P4, ("g_x0" if pc % 2 == 0 else "g_x1")) for pc in range(1, 4)]
        xgate = {}
        cnts = {"g_x0": 0, "g_x1": 0, "g_xf": 0}
        for t0, ntile, semn in xpieces:
            xin = bass.AP(x, t0 * 128 * E, [[E, 128], [128 * E, ntile], [1, E]])
            slot = (t0 % 8)
            xout = xrA[:, slot * 1024: slot * 1024 + ntile * 1024]
            p.wait("gpsimd", "t_tp", t0 - P4)
            p.op("gpsimd", lambda g, a=xout, b=xin: g.dma_start(a, b), inc=semn, n=16)
            cnts[semn] += 16
            for t in range(t0, t0 + ntile):
                xgate[t] = (semn, cnts[semn])

        def wq_dma(ocbase, slot):
            src = bass.AP(wqkv, ocbase * 128,
                          [[3 * E, 128], [128 * 3 * E, 8], [1, 1024]])
            return lambda g, a=wqs[slot][:, :], b=src: g.dma_start(a, b)
        p.op("gpsimd", wq_dma(0, 0), inc="g_wq0", n=16)
        p.op("gpsimd", wq_dma(8, 1), inc="g_wq1", n=16)
        p.wait("gpsimd", "t_qk", 8)
        p.op("gpsimd", wq_dma(16, 0), inc="g_wq0", n=16)

        p.op("sync", lambda s: s.dma_start(sb["bqk_sb"][:], bqk[:, :]), inc="s_cst", n=16)
        p.op("sync", lambda s: s.dma_start(sb["gbb_sb"][:], gbb[:, :]), inc="s_cst", n=16)
        p.op("sync", lambda s: s.dma_start(sb["ident_sb"][:], ident[:, :]), inc="s_cst", n=16)
        P8 = T // 8
        for pc in range(8):
            t0 = pc * P8
            xin = bass.AP(x, t0 * 128 * E, [[E, 128], [128 * E, P8], [1, E]])
            dst = RES[:, t0 * 1024:(t0 + P8) * 1024]
            semn = X0SEMS[pc]
            p.wait("sync", "t_s", (8 * (pc % 4) + 8) * T)
            p.op("sync", lambda s, a=dst, b=xin: s.dma_start(a, b), inc=semn, n=16)

        for t in range(T):
            p.wait("tensor", "s_cst", 48)
            p.wait("tensor", xgate[t][0], xgate[t][1])
            p.wait("tensor", "v_tp", t - 1)
            slot = t % 8
            for c in range(8):
                p.op("tensor",
                     lambda te, a=tp[t % 2][:, c * 128:(c + 1) * 128],
                            b=xrA[:, slot * 1024 + c * 128: slot * 1024 + (c + 1) * 128],
                            i=sb["ident_sb"][:, :]:
                        te.matmul(a, b, i, is_transpose=True, start=True, stop=True),
                     inc=("t_tp" if c == 7 else None))
        for t in range(T):
            p.wait("vector", "t_tp", t + 1)
            dst = bass.AP(XF, t * 128, [[8 * S, 128], [S, 8], [1, 128]])
            p.op("vector", lambda v, a=dst, b=tp[t % 2][:, 0:1024]: v.tensor_copy(a, b),
                 inc="v_tp")

        for oc in range(16):
            piece = oc // 8
            if oc == 0:
                p.wait("tensor", "g_wq0", 16)
            if oc == 8:
                p.wait("tensor", "g_wq1", 16)
            p.wait("tensor", "a_qk", oc - 1)
            for g in range(S // 512):
                p.wait("tensor", "v_tp", 4 * (g + 1))
                if oc % 2 == 1 and g >= 2:
                    p.wait("tensor", "v_tp", T - 3 + g)
                for kc in range(8):
                    lhs = wqs[piece][:, kc * 1024 + (oc % 8) * 128: kc * 1024 + (oc % 8) * 128 + 128]
                    p.op("tensor",
                         lambda te, a=qk[oc % 2][:, g * 512:(g + 1) * 512], l=lhs,
                                r=XF[:, kc * S + g * 512: kc * S + (g + 1) * 512],
                                st=(kc == 0), sp=(kc == 7):
                            te.matmul(a, l, r, start=st, stop=sp),
                         inc=("t_qk" if (kc == 7 and g == S // 512 - 1) else None))
        for oc in range(16):
            p.wait("scalar", "t_qk", oc + 1)
            p.wait("scalar", "s_cst", 48)
            dstT = QF if oc < 8 else KF
            dst = dstT[:, (oc % 8) * S:(oc % 8 + 1) * S]
            p.op("scalar",
                 lambda sc, a=dst, b=qk[oc % 2][:, 0:S], bias=sb["bqk_sb"][:, oc:oc + 1]:
                    sc.activation(a, b, AF.Identity, bias=bias, scale=1.0),
                 inc="a_qk")
        for tt in range(T):
            if tt == 0:
                p.wait("tensor", "a_qk", 16)
                p.wait("tensor", "g_wq0", 32)
                p.wait("tensor", "g_ms", 3)
                p.wait("tensor", "g_cst", 48)
            p.wait("tensor", "v_v", tt - 1)
            for kc in range(8):
                lhs = XF[:, kc * S + tt * 128: kc * S + (tt + 1) * 128]
                for g in range(2):
                    rhs = wqs[0][:, kc * 1024 + g * 512: kc * 1024 + (g + 1) * 512]
                    p.op("tensor",
                         lambda te, a=vp[tt % 2][:, g * 512:(g + 1) * 512], l=lhs, r=rhs,
                                st=(kc == 0):
                            te.matmul(a, l, r, start=st, stop=False))
            for g in range(2):
                p.op("tensor",
                     lambda te, a=vp[tt % 2][:, g * 512:(g + 1) * 512],
                            l=sb["ones"][0:1, 0:128], r=sb["bv_sb"][0:1, g * 512:(g + 1) * 512]:
                        te.matmul(a, l, r, start=False, stop=True),
                     inc=("t_v" if g == 1 else None))
        for tt in range(T):
            p.wait("vector", "t_v", tt + 1)
            dst = bass.AP(VT, tt * H * 65, [[T * H * 65, 128], [65, H], [1, 64]])
            p.op("vector", lambda v, a=dst, b=vp[tt % 2][:, 0:1024]: v.tensor_copy(a, b),
                 inc="v_v")

        def w2_dma(e):
            src = bass.AP(w2, e * INTER * E,
                          [[E, 128], [128 * E, 4], [1, E]])
            return lambda g, a=w2sb[e % 2], b=src: g.dma_start(a[:, :], b)
        p.wait("gpsimd", "t_qk", 16)
        p.op("gpsimd", w2_dma(1), inc="g_w2b", n=16)
        p.wait("gpsimd", "t_v", T)
        p.op("gpsimd", lambda g: g.dma_start(
            woutsb[:, :], bass.AP(wout, 0, [[E, 128], [128 * E, 8], [1, E]])),
            inc="g_wo", n=16)
        p.op("gpsimd", w2_dma(0), inc="g_w2a", n=16)

        NG = 32 * T

        def attn_evac(sqa):
            hpa, qqa = sqa // 4, sqa % 4
            p.wait("vector", "t_av", (sqa + 1) * T)
            if sqa >= 4:
                p.wait("vector", "v_div", sqa - 3)
                p.wait("vector", "g_rb", 32 * (sqa - 3))
            for hx in range(2):
                po = hx * 64
                p.op("vector", lambda v, a=sb["Ost"][po:po + 64, qqa * NQ:(qqa + 1) * NQ],
                            b=O2[sqa % 2][hx][0:64, 0:NQ]: v.tensor_copy(a, b),
                     inc="v_os", lp="softmax denominator rounded to bf16")
                p.op("vector", lambda v, a=sb["sums"][0:1, hx * S + qqa * NQ: hx * S + (qqa + 1) * NQ],
                            b=O2[sqa % 2][hx][64:65, 0:NQ]: v.tensor_copy(a, b),
                     inc="v_os", lp="softmax denominator rounded to bf16")
            p.wait("gpsimd", "v_os", 4 * (sqa + 1))
            if sqa >= 4:
                p.wait("gpsimd", "v_div", sqa - 3)
            p.op("gpsimd", lambda g, a=sb["rb"][0:64, qqa * NQ:(qqa + 1) * NQ],
                        b=bass.AP(sb["sums"], qqa * NQ, [[2 * S, 1], [0, 64], [1, NQ]]):
                 g.dma_start(a, b), inc="g_rb", n=16)
            p.op("gpsimd", lambda g, a=sb["rb"][64:128, qqa * NQ:(qqa + 1) * NQ],
                        b=bass.AP(sb["sums"], S + qqa * NQ, [[2 * S, 1], [0, 64], [1, NQ]]):
                 g.dma_start(a, b), inc="g_rb", n=16)
            p.wait("vector", "g_rb", 32 * (sqa + 1))
            p.op("vector", lambda v, a=sb["rb"][:, qqa * NQ:(qqa + 1) * NQ]:
                 v.reciprocal(a, a), inc="v_rc", lp="softmax denominator rounded to bf16")
            p.wait("vector", "v_rc", sqa + 1)
            p.op("vector",
                 lambda v, a=attnF[:, hpa * S + qqa * NQ: hpa * S + (qqa + 1) * NQ],
                        b=sb["Ost"][:, qqa * NQ:(qqa + 1) * NQ],
                        c=sb["rb"][:, qqa * NQ:(qqa + 1) * NQ]:
                    v.tensor_tensor(a, b, c, op=OP.mult),
                 inc="v_div")

        for gk in range(NG + 2):
            if gk >= 2:
                av = gk - 2
                sqa, ktv = av // T, av % T
                hpa = sqa // 4
                heads = (2 * hpa, 2 * hpa + 1)
                if ktv == 0 and sqa >= 2:
                    p.wait("tensor", "v_os", 4 * (sqa - 1))
                p.wait("tensor", "a_e", av + 1)
                for hx in range(2):
                    p.op("tensor",
                         lambda te, a=O2[sqa % 2][hx][0:65, 0:NQ],
                                l=VT[:, ktv * H * 65 + heads[hx] * 65: ktv * H * 65 + heads[hx] * 65 + 65],
                                r=probsP[ktv % 2][:, hx * NQ:(hx + 1) * NQ],
                                st=(ktv == 0), sp=(ktv == T - 1):
                            te.matmul(a, l, r, start=st, stop=sp),
                         inc=("t_av" if hx == 1 else None))
                if ktv == T - 1:
                    attn_evac(sqa)
            if gk < NG:
                sqi, kt = gk // T, gk % T
                hp, qq = sqi // 4, sqi % 4
                if gk == 0:
                    p.wait("tensor", "v_v", T)
                if gk >= 2:
                    p.wait("tensor", "a_e", gk - 1)
                for hx in range(2):
                    po = hx * 64
                    p.op("tensor",
                         lambda te, a=Sq[kt % 2][:, hx * NQ:(hx + 1) * NQ],
                                l=KF[po:po + 64, hp * S + kt * 128: hp * S + (kt + 1) * 128],
                                r=QF[po:po + 64, hp * S + qq * NQ: hp * S + (qq + 1) * NQ]:
                            te.matmul(a, l, r, start=True, stop=True),
                         inc=("t_s" if hx == 1 else None))
                if gk == 0:
                    p.wait("scalar", "t_v", T)
                p.wait("scalar", "t_s", gk + 1)
                p.op("scalar",
                     lambda sc, a=probsP[kt % 2][:, 0:2 * NQ], b=Sq[kt % 2][:, 0:2 * NQ]:
                        sc.activation(a, b, AF.Exp, scale=0.125),
                     inc="a_e")

        def w13_dma(e):
            src = bass.AP(w13, e * E * 2 * INTER,
                          [[2 * INTER, 128], [128 * 2 * INTER, 8], [1, 1024]])
            return lambda g, a=w13sb[e % 2][:, :], b=src: g.dma_start(a, b)
        p.wait("gpsimd", "t_av", 32 * T)
        p.op("gpsimd", w13_dma(0), inc="g_w13a", n=16)
        p.op("gpsimd", w13_dma(1), inc="g_w13b", n=16)

        for e in range(2, NX if not stop_after else 0):
            p.wait("gpsimd", "t_h13", (e - 1) * 4 * TG)
            p.op("gpsimd", w13_dma(e), inc=("g_w13a" if e % 2 == 0 else "g_w13b"), n=16)
            p.wait("gpsimd", "t_w2", (e - 1) * T)
            p.setfloor("gpsimd", "t_h13", (e - 1) * 4 * TG)
            p.op("gpsimd", w2_dma(e), inc=("g_w2a" if e % 2 == 0 else "g_w2b"), n=16)

        ln_rdy = {}

        def applycast(tt):
            rtc = RES[:, tt * 1024:(tt + 1) * 1024]
            p.wait("scalar", "v_ln", ln_rdy[tt])
            p.wait("scalar", "t_tp2", tt - 3)
            p.op("scalar", lambda sc, a=xrB[:, (tt % 4) * 1024:((tt % 4) + 1) * 1024],
                        b=rtc, t=tt: sc.activation(a, b, AF.Identity,
                        bias=sb["nm"][:, t:t + 1], scale=sb["rs"][:, t:t + 1]),
                 inc="a_xc")
        for tt in range(T):
            if tt == 0:
                p.wait("tensor", "g_wo", 16)
                p.wait("tensor", "a_e", p.cnt["a_e"])
                if NQ < 512:
                    p.wait("tensor", "v_os", p.cnt["v_os"])
                for s_ in ("t_s", "t_av", "v_os", "v_rc"):
                    p.setfloor("tensor", s_, p.cnt[s_])
            p.wait("tensor", "v_res", tt - 1)
            for kc in range(8):
                p.wait("tensor", "v_div", 4 * kc + tt // 4 + 1)
                lhs = attnF[:, kc * S + tt * 128: kc * S + (tt + 1) * 128]
                for j in range(2):
                    p.op("tensor",
                         lambda te, a=yp[tt % 2][:, j * 512:(j + 1) * 512], l=lhs,
                                r=woutsb[:, kc * 1024 + j * 512: kc * 1024 + (j + 1) * 512],
                                st=(kc == 0):
                            te.matmul(a, l, r, start=st, stop=False))
            for j in range(2):
                p.op("tensor",
                     lambda te, a=yp[tt % 2][:, j * 512:(j + 1) * 512],
                            l=sb["ones"][0:1, 0:128], r=sb["bout_sb"][0:1, j * 512:(j + 1) * 512]:
                        te.matmul(a, l, r, start=False, stop=True),
                     inc=("t_y" if j == 1 else None))
        for tt in range(T):
            p.wait("vector", "t_y", tt + 1)
            p.wait("vector", X0SEMS[tt // (T // 8)], 16)
            rt = RES[:, tt * 1024:(tt + 1) * 1024]
            p.op("vector", lambda v, a=rt, b=yp[tt % 2][:, 0:1024]: v.tensor_tensor(a, b, a, op=OP.add),
                 inc="v_res")
            p.wait("vector", "v_res", tt + 1)
            p.wait("vector", "v_ln", p.cnt["v_ln"])
            p.op("vector", lambda v, a=rt: v.bn_stats(sb["stats"][:, 0:6], a[:, 0:512]), inc="v_ln")
            p.op("vector", lambda v, a=rt: v.bn_stats(sb["stats"][:, 6:12], a[:, 512:1024]), inc="v_ln")
            p.wait("vector", "v_ln", p.cnt["v_ln"])
            p.op("vector", lambda v, t=tt: v.bn_aggr(sb["mv"][:, 2 * t:2 * t + 2], sb["stats"][:, 0:12]),
                 inc="v_ln")
            p.wait("scalar", "v_ln", p.cnt["v_ln"])
            p.wait("scalar", "g_ms", 2)
            p.op("scalar", lambda sc, t=tt: sc.activation(
                sb["sq"][:, t:t + 1], sb["mv"][:, 2 * t + 1:2 * t + 2], AF.Sqrt,
                bias=sb["eps"][:, 0:1]), inc="a_sq")
            p.wait("vector", "a_sq", tt + 1)
            p.op("vector", lambda v, t=tt: v.reciprocal(sb["rs"][:, t:t + 1], sb["sq"][:, t:t + 1]),
                 inc="v_ln")
            p.wait("vector", "v_ln", p.cnt["v_ln"])
            p.op("vector", lambda v, t=tt: v.tensor_scalar(
                sb["nm"][:, t:t + 1], sb["mv"][:, 2 * t:2 * t + 1], sb["rs"][:, t:t + 1],
                -1.0, op0=OP.mult, op1=OP.mult), inc="v_ln")
            ln_rdy[tt] = p.cnt["v_ln"]
            if tt == T // 2 - 1:
                for k_ in range(min(4, T // 2)):
                    applycast(k_)
        for tt in range(T):
            rt = RES[:, tt * 1024:(tt + 1) * 1024]
            p.wait("vector", "a_xc", tt + 1)
            p.op("vector", lambda v, a=rt, t=tt: v.tensor_scalar(
                a, a, sb["mv"][:, 2 * t:2 * t + 1], sb["rs"][:, t:t + 1],
                op0=OP.subtract, op1=OP.mult), inc="v_ap")
        def xf_evac(k):
            p.wait("scalar", "t_tp2", k + 1)
            dst = bass.AP(XF, k * 128, [[8 * S, 128], [S, 8], [1, 128]])
            p.op("scalar", lambda sc, a=dst, b=tq[k % 2][:, 0:1024]: sc.copy(a, b),
                 inc="v_tp2")

        for tt in range(min(4, T // 2), T):
            applycast(tt)
            if tt >= 4:
                xf_evac(tt - 4)
        for k_ in range(T - 4, T):
            xf_evac(k_)
        if not stop_after:
            QT = T // 4
            for q in range(4):
                p.wait("scalar", "t_g", QT * (q + 1))
                p.op("scalar", lambda sc, c0=q * QT * NE, c1=(q + 1) * QT * NE:
                     sc.activation(sb["scoresg"][:, c0:c1], gpq[:, c0:c1], AF.Sigmoid),
                     inc="a_g")
        def gate_mm(gt):
            p.wait("tensor", "v_tp2", gt + 1)
            for kc in range(8):
                p.op("tensor",
                     lambda te, a=gpq[:, gt * NE:(gt + 1) * NE],
                            l=XF[:, kc * S + gt * 128: kc * S + (gt + 1) * 128],
                            r=sb["gw_sb"][:, kc * NE:(kc + 1) * NE],
                            st=(kc == 0), sp=(kc == 7):
                        te.matmul(a, l, r, start=st, stop=sp),
                     inc=("t_g" if kc == 7 else None))

        for tt in range(T):
            p.wait("tensor", "a_xc", tt + 1)
            p.wait("tensor", "v_tp2", tt - 1)
            for c in range(8):
                p.op("tensor",
                     lambda te, a=tq[tt % 2][:, c * 128:(c + 1) * 128],
                            b=xrB[:, (tt % 4) * 1024 + c * 128:(tt % 4) * 1024 + (c + 1) * 128],
                            i=sb["ident_sb"][:, :]:
                        te.matmul(a, b, i, is_transpose=True, start=True, stop=True),
                     inc=("t_tp2" if c == 7 else None))
            if not stop_after and tt >= 3:
                gate_mm(tt - 3)
        if not stop_after:
            for gt in range(T - 3, T):
                gate_mm(gt)



        vtk_after = 0

        def topk_all():
            p.wait("vector", "a_g", 4)
            p.wait("vector", "s_cst", 48)
            SG = sb["scoresg"]
            W, M, m = sb["tkW"], sb["tkM"], sb["tkm"]
            seg = lambda t_: bass.AP(t_, 0, [[NE * T, 128], [NE, T], [1, NE]])
            mb = bass.AP(m, 0, [[T, 128], [1, T], [0, NE]])
            gbbb = bass.AP(sb["gbb_sb"], 0, [[NE, 128], [0, T], [1, NE]])
            B = sb["combT"]

            def vtk(fn):
                p.op("vector", fn, inc="v_tk")
                p.wait("vector", "v_tk", p.cnt["v_tk"])

            vtk(lambda v: v.tensor_tensor(B[:], SG[:], gbbb, op=OP.add))
            vtk(lambda v: v.tensor_copy(W[:], B[:]))
            for it in range(TOPK - 1):
                vtk(lambda v: v.reduce_max(m[:, 0:T], seg(W), axis=AX.X))
                vtk(lambda v: v.tensor_tensor(M[:], W[:], mb, op=OP.is_ge))
                vtk(lambda v: v.scalar_tensor_tensor(W[:], M[:], -BIG,
                                                     W[:], op0=OP.mult, op1=OP.add))
            vtk(lambda v: v.reduce_max(m[:, 0:T], seg(W), axis=AX.X))
            vtk(lambda v: v.tensor_tensor(M[:], B[:], mb, op=OP.is_ge))
            vtk(lambda v: v.tensor_tensor(W[:], SG[:], M[:], op=OP.mult))
            vtk(lambda v: v.reduce_sum(m[:, 0:T], seg(W), axis=AX.X))
            vtk(lambda v: v.reciprocal(m[:, 0:T], m[:, 0:T]))
            vtk(lambda v: v.scalar_tensor_tensor(B[:], W[:], ROUTE_SCALE,
                                                 mb, op0=OP.mult, op1=OP.mult))

        def ln2_tile(tt):
            p.wait("vector", "v_y", NE * T + tt + 1)
            rt = RES[:, tt * 1024:(tt + 1) * 1024]
            if tt == 0:
                p.wait("vector", "v_ln", 4 * T)
            p.wait("vector", "v_ln2", p.cnt["v_ln2"])
            p.op("vector", lambda v, a=rt: v.bn_stats(sb["stats"][:, 0:6], a[:, 0:512]), inc="v_ln2")
            p.op("vector", lambda v, a=rt: v.bn_stats(sb["stats"][:, 6:12], a[:, 512:1024]), inc="v_ln2")
            p.wait("vector", "v_ln2", p.cnt["v_ln2"])
            p.op("vector", lambda v, t=tt: v.bn_aggr(sb["mv2"][:, 2 * t:2 * t + 2], sb["stats"][:, 0:12]),
                 inc="v_ln2")
            p.wait("scalar", "v_ln2", p.cnt["v_ln2"])
            p.op("scalar", lambda sc, t=tt: sc.activation(
                sb["sq2"][:, t:t + 1], sb["mv2"][:, 2 * t + 1:2 * t + 2], AF.Sqrt,
                bias=sb["eps"][:, 0:1]), inc="a_sq2")
            p.wait("vector", "a_sq2", tt + 1)
            p.op("vector", lambda v, t=tt: v.reciprocal(sb["rs2"][:, t:t + 1], sb["sq2"][:, t:t + 1]),
                 inc="v_ln2")
            p.wait("vector", "v_ln2", p.cnt["v_ln2"])
            p.wait("vector", ("s_y0" if tt % 2 == 0 else "s_y1"), 16 * (tt // 2))
            p.op("vector", lambda v, a=outr[:, (tt % 2) * 1024:((tt % 2) + 1) * 1024],
                        b=rt, t=tt: v.tensor_scalar(
                a, b, sb["mv2"][:, 2 * t:2 * t + 1], sb["rs2"][:, t:t + 1],
                op0=OP.subtract, op1=OP.mult), inc="v_o")
            p.wait("sync", "v_o", tt + 1)
            p.op("sync", lambda s, t=tt: s.dma_start(
                y[t * 128:(t + 1) * 128, :], outr[:, (t % 2) * 1024:((t % 2) + 1) * 1024]),
                inc=("s_y0" if tt % 2 == 0 else "s_y1"), n=16)

        if not stop_after:
            topk_all()
            vtk_after = p.cnt["v_tk"]
        for e in range(NX if not stop_after else 0):
            for tg in range(TG):
                for pp in range(4):
                    u = e * 4 * TG + tg * 4 + pp
                    if pp == 0 and tg == 0:
                        p.wait("tensor", ("g_w13a" if e % 2 == 0 else "g_w13b"),
                               16 * (e // 2 + 1))
                        if e == 0:
                            p.wait("tensor", "v_res", T)
                    if e == 0:
                        p.wait("tensor", "v_tp2", 4 * tg + 4)
                    p.wait("tensor", "v_h", u - 1)
                    for kc in range(8):
                        lg = w13sb[e % 2][:, kc * 1024 + pp * 128: kc * 1024 + (pp + 1) * 128]
                        lu = w13sb[e % 2][:, kc * 1024 + (pp + 4) * 128: kc * 1024 + (pp + 5) * 128]
                        rhs = XF[:, kc * S + tg * 512: kc * S + (tg + 1) * 512]
                        p.op("tensor", lambda te, a=hg[pp % 2][:, 0:512], l=lg, r=rhs,
                                    st=(kc == 0), sp=(kc == 7):
                             te.matmul(a, l, r, start=st, stop=sp))
                        p.op("tensor", lambda te, a=hu[pp % 2][:, 0:512], l=lu, r=rhs,
                                    st=(kc == 0), sp=(kc == 7):
                             te.matmul(a, l, r, start=st, stop=sp),
                             inc=("t_h13" if kc == 7 else None))
            for tg in range(TG):
                for pp in range(4):
                    u = e * 4 * TG + tg * 4 + pp
                    p.wait("scalar", "t_h13", u + 1)
                    p.wait("scalar", "v_h", u - (NSL - 1))
                    p.op("scalar", lambda sc, a=sb["silu_sb"][:, (u % NSL) * 512:((u % NSL) + 1) * 512],
                                b=hg[pp % 2][:, 0:512]: sc.activation(a, b, AF.Silu),
                         inc="a_si")
                    p.wait("vector", "a_si", u + 1)
                    if pp == 0 and tg == 0:
                        p.wait("vector", "t_w2", (e - 1) * T)
                    p.op("vector", lambda v, a=hsb[e % 2][:, pp * S + tg * 512: pp * S + (tg + 1) * 512],
                                b=sb["silu_sb"][:, (u % NSL) * 512:((u % NSL) + 1) * 512],
                                c=hu[pp % 2][:, 0:512]:
                            v.tensor_tensor(a, b, c, op=OP.mult),
                         inc="v_h")

            for tt in range(T):
                if tt == 0:
                    p.wait("tensor", ("g_w2a" if e % 2 == 0 else "g_w2b"), 16 * (e // 2 + 1))
                p.wait("tensor", "v_h", e * 4 * TG + (tt // 4) * 4 + 4)
                p.wait("tensor", "v_y", e * T + tt - 1)
                if e == 0 and tt <= 1:
                    if tt == 0:
                        p.setfloor("tensor", "v_tp2", T)
                    if tt == 1:
                        p.wait("tensor", "a_g", 4)
                for oc in range(4):
                    lhs = hsb[e % 2][:, oc * S + tt * 128: oc * S + (tt + 1) * 128]
                    for j in range(2):
                        p.op("tensor", lambda te, a=y2[tt % 2][:, j * 512:(j + 1) * 512], l=lhs,
                                    r=w2sb[e % 2][:, oc * 1024 + j * 512: oc * 1024 + (j + 1) * 512],
                                    st=(oc == 0), sp=(oc == 3):
                                te.matmul(a, l, r, start=st, stop=sp),
                             inc=("t_w2" if (oc == 3 and j == 1) else None))
            for tt in range(T):
                p.wait("vector", "t_w2", e * T + tt + 1)
                if e == 0:
                    p.wait("vector", "v_tk", vtk_after)
                if e == 0:
                    p.wait("vector", "a_xc", tt + 1)
                p.wait("vector", "v_y", e * T + tt)
                rt = RES[:, tt * 1024:(tt + 1) * 1024]
                if e < NE:
                    p.op("vector", lambda v, a=rt, t=tt, ee=e: v.scalar_tensor_tensor(
                        a, y2[t % 2][:, 0:1024], sb["combT"][:, t * NE + ee: t * NE + ee + 1], a,
                        op0=OP.mult, op1=OP.add), inc="v_y")
                else:
                    p.op("vector", lambda v, a=rt, t=tt: v.tensor_tensor(
                        a, y2[t % 2][:, 0:1024], a, op=OP.add), inc="v_y")
                    ln2_tile(tt)

        if not stop_after:
            p.wait("sync", "s_y0", 16 * (T // 2))
            p.wait("sync", "s_y1", 16 * (T // 2))
        else:
            p.wait("sync", "v_tp2", T)

        dbg_map = {"res": RES, "comb": sb["combT"], "scoresg": sb["scoresg"],
                   "xf": XF, "attnf": attnF, "qf": QF, "kf": KF, "vt": VT}
        for name in debug:
            h_ = dbg_map[name]
            dt_np = F32 if h_.dtype == F32 else BF16
            d = nc.dram_tensor("dbg_" + name, list(h_.shape), dt_np, kind="ExternalOutput")
            dbg_tensors[name] = "dbg_" + name
            p.op("sync", lambda s, a=d, b=h_: s.dma_start(a[:, :], b[:]), inc="s_dbg", n=16)
        if debug:
            p.wait("sync", "s_dbg", 16 * len(debug))

        block = EC(nc.Block())

        def replay(engname):
            def body(eng):
                for item in p.q[engname]:
                    if item[0] == "w":
                        eng.wait_ge(sems[item[1]], item[2])
                    else:
                        _, fn, inc, n, lp = item
                        if lp:
                            with nc.allow_low_precision(reason=lp):
                                ins = fn(eng)
                        else:
                            ins = fn(eng)
                        if inc is not None:
                            ins.then_inc(sems[inc], n)
            return body

        block.sync(replay("sync"))
        block.gpsimd(replay("gpsimd"))
        block.tensor(replay("tensor"))
        block.vector(replay("vector"))
        block.scalar(replay("scalar"))

    return nc, dbg_tensors



def _prep_weights(Wqkv, bqkv, Wout, bout_, gate_w, gate_b, w13_, w2_, w13s, w2s):
    E_, H_, HD_ = 1024, 16, 64
    qc = np.concatenate([np.arange(h * 192, h * 192 + 64) for h in range(H_)])
    kc = qc + 64
    vc = qc + 128
    perm = np.concatenate([qc, kc, vc])
    wqkv_r = np.ascontiguousarray(Wqkv[:, perm], dtype=np.float32)
    bqkv_r = np.ascontiguousarray(bqkv[perm], dtype=np.float32)
    bqk = np.ascontiguousarray(bqkv_r[:2048].reshape(16, 128).T, dtype=np.float32)
    bv_ = np.ascontiguousarray(bqkv_r[2048:].reshape(1, 1024), dtype=np.float32)
    w13T = np.ascontiguousarray(
        np.concatenate([np.transpose(w13_, (0, 2, 1)),
                        w13s.T[None, :, :]], axis=0), dtype=np.float32)
    w2T = np.ascontiguousarray(
        np.concatenate([np.transpose(w2_, (0, 2, 1)),
                        w2s.T[None, :, :]], axis=0), dtype=np.float32)
    return {
        "wqkv": wqkv_r,
        "bqk": bqk,
        "bv": bv_,
        "wout": np.ascontiguousarray(Wout, dtype=np.float32),
        "bout": np.ascontiguousarray(np.asarray(bout_).reshape(1, 1024), dtype=np.float32),
        "gw": np.ascontiguousarray(gate_w.T, dtype=np.float32),
        "gbb": np.ascontiguousarray(np.tile(np.asarray(gate_b).reshape(1, 16), (128, 1)),
                                    dtype=np.float32),
        "w13": w13T,
        "w2": w2T,
        "ident": np.eye(128, dtype=ml_dtypes.bfloat16),
    }


def kernel(L0_attn_x, L0_attn_Wqkv, L0_attn_bqkv, L0_attn_Wout, L0_attn_bout,
           L0_ffn_input_ids, L0_ffn_gate_w, L0_ffn_gate_b, L0_ffn_w13,
           L0_ffn_w2, L0_ffn_w13_s, L0_ffn_w2_s):
    xx = np.ascontiguousarray(np.asarray(L0_attn_x), dtype=np.float32)
    B = xx.shape[0]
    weights = _prep_weights(
        np.asarray(L0_attn_Wqkv), np.asarray(L0_attn_bqkv),
        np.asarray(L0_attn_Wout), np.asarray(L0_attn_bout),
        np.asarray(L0_ffn_gate_w), np.asarray(L0_ffn_gate_b),
        np.asarray(L0_ffn_w13), np.asarray(L0_ffn_w2),
        np.asarray(L0_ffn_w13_s), np.asarray(L0_ffn_w2_s))

    nc, _dbg = build(S=2048)
    in_maps = []
    for c in range(B):
        m = dict(weights)
        m["x"] = np.ascontiguousarray(xx[c])
        in_maps.append(m)
    res = run_bass_kernel_spmd(nc, in_maps, core_ids=list(range(B)))
    out = np.stack([np.asarray(res.results[c]["y"]) for c in range(B)], axis=0)
    return out.astype(np.float32)


# version 32
# speedup vs baseline: 1.0013x; 1.0013x over previous
import numpy as np
import ml_dtypes
import concourse.bass as bass
import concourse.mybir as mybir
from concourse.bass_utils import run_bass_kernel_spmd
from contextlib import ExitStack

F32 = mybir.dt.float32
BF16 = mybir.dt.bfloat16
AF = mybir.ActivationFunctionType
OP = mybir.AluOpType
AX = mybir.AxisListType

E = 1024
H = 16
HD = 64
NE = 16
NX = NE + 1
INTER = 512
TOPK = 4
ROUTE_SCALE = 2.5
LN_EPS = 1e-5
BIG = 1.0e30

ENGS = ("sync", "gpsimd", "tensor", "vector", "scalar")


class Plan:
    def __init__(self):
        self.q = {e: [] for e in ENGS}
        self.cnt = {}
        self.floor = {}

    def sem(self, name):
        self.cnt[name] = 0
        return name

    def wait(self, eng, sem, val):
        if val <= 0:
            return
        key = (eng, sem)
        if self.floor.get(key, 0) >= val:
            return
        self.floor[key] = val
        self.q[eng].append(("w", sem, val))

    def setfloor(self, eng, sem, val):
        key = (eng, sem)
        if self.floor.get(key, 0) < val:
            self.floor[key] = val

    def op(self, eng, fn, inc=None, n=1, lp=None):
        self.q[eng].append(("o", fn, inc, n, lp))
        if inc is not None:
            self.cnt[inc] += n
            return self.cnt[inc]
        return None


def build(S=2048, debug=(), stop_after=None):
    T = S // 128
    TG = S // 512
    NQ = S // 4
    NJ = max(1, NQ // 512)
    assert NJ >= 1 and S % 512 == 0
    P4 = T // 4

    nc = bass.Bass("TRN2", target_bir_lowering=False)

    x = nc.dram_tensor("x", [S, E], F32, kind="ExternalInput")
    wqkv = nc.dram_tensor("wqkv", [E, 3 * E], F32, kind="ExternalInput")
    bqk = nc.dram_tensor("bqk", [128, 16], F32, kind="ExternalInput")
    bv = nc.dram_tensor("bv", [1, E], F32, kind="ExternalInput")
    wout = nc.dram_tensor("wout", [E, E], F32, kind="ExternalInput")
    bout = nc.dram_tensor("bout", [1, E], F32, kind="ExternalInput")
    gw = nc.dram_tensor("gw", [E, NE], F32, kind="ExternalInput")
    gbb = nc.dram_tensor("gbb", [128, NE], F32, kind="ExternalInput")
    w13 = nc.dram_tensor("w13", [NX, E, 2 * INTER], F32, kind="ExternalInput")
    w2 = nc.dram_tensor("w2", [NX, INTER, E], F32, kind="ExternalInput")
    ident = nc.dram_tensor("ident", [128, 128], BF16, kind="ExternalInput")
    y = nc.dram_tensor("y", [S, E], F32, kind="ExternalOutput")

    dbg_tensors = {}

    X0SEMS = ("s_x0a", "s_x0b", "s_x0c", "s_x0d", "s_x0e", "s_x0f", "s_x0g", "s_x0h")
    p = Plan()
    for s in ("s_cst", "g_cst", "g_x0", "g_x1", "g_xf", "g_wq0", "g_wq1", "g_wo", "g_rb",
              "g_w13a", "g_w13b", "g_w2a", "g_w2b",
              "s_x0a", "s_x0b", "s_x0c", "s_x0d",
              "s_x0e", "s_x0f", "s_x0g", "s_x0h", "s_y0", "s_y1", "s_dbg"):
        p.sem(s)
    for s in ("g_ms", "t_tp", "v_tp", "t_qk", "a_qk", "t_v", "v_v",
              "t_s", "a_e", "t_av", "v_rc", "v_os", "v_div",
              "t_y", "v_res", "v_ln", "a_sq", "v_ap", "a_xc",
              "t_tp2", "v_tp2", "t_g", "a_g", "v_tk",
              "t_h13", "a_si", "v_h", "t_w2", "v_y",
              "v_ln2", "a_sq2", "v_o"):
        p.sem(s)

    with ExitStack() as ctx:
        EC = ctx.enter_context

        SZ_XF = 8 * S * 2
        SZ_QF = 8 * S * 2
        SZ_KF = 8 * S * 2
        SZ_VT = max(T * H * 65 * 2, 32768)
        SZ_AF = 8 * S * 2
        SZ_WQS = 8 * 1024 * 2
        off = {}
        o = 0
        for name, sz in (("XF", SZ_XF), ("QF", SZ_QF), ("KF", SZ_KF),
                         ("VT", SZ_VT), ("attnF", SZ_AF),
                         ("wqs0", SZ_WQS), ("wqs1", SZ_WQS)):
            off[name] = o
            o += (sz + 31) // 32 * 32
        misc = [("ident_sb", 128, 128, BF16), ("ones", 1, 128, BF16),
                ("bqk_sb", 128, 16, F32), ("gbb_sb", 128, NE, F32),
                ("gw_sb", 128, 8 * NE, BF16), ("bv_sb", 1, E, BF16),
                ("bout_sb", 1, E, BF16),
                ("scoresg", 128, NE * T, F32), ("combT", 128, NE * T, F32),
                ("stats", 128, 12, F32), ("mv", 128, 2 * T, F32),
                ("sq", 128, T, F32), ("rs", 128, T, F32),
                ("tkW", 128, NE * T, F32),
                ("tkM", 128, NE * T, F32), ("tkm", 128, T, F32),

                ("mv2", 128, 2 * T, F32), ("sq2", 128, T, F32), ("rs2", 128, T, F32),
                ("eps", 128, 1, F32), ("nm", 128, 16, F32)]
        for name, pd, fd, dt in misc:
            sz = fd * (4 if dt == F32 else 2)
            off[name] = o
            o += (sz + 31) // 32 * 32
        total = o

        rng_ = nc.bump_sbuf(total)
        assert rng_ is not None, f"SBUF over budget: need {total} bytes/partition"
        base0, _end0 = rng_

        def at(name, shape, dt, rel):
            return nc.alloc_sbuf_tensor_at(name, shape, dt, offset=base0 + rel)

        XF = at("XF", [128, 8 * S], BF16, off["XF"])
        QF = at("QF", [128, 8 * S], BF16, off["QF"])
        KF = at("KF", [128, 8 * S], BF16, off["KF"])
        VT = at("VT", [128, T * H * 65], BF16, off["VT"])
        attnF = at("attnF", [128, 8 * S], BF16, off["attnF"])
        wqs = [at("wqs0", [128, 8192], BF16, off["wqs0"]),
               at("wqs1", [128, 8192], BF16, off["wqs1"])]
        sb = {}
        for name, pd, fd, dt in misc:
            sb[name] = at(name, [pd, fd], dt, off[name])
        RES = at("RES", [128, T * 1024], F32, off["QF"])
        woutsb = at("woutsb", [128, 8192], BF16, off["XF"])
        w13sb = [at("w13a", [128, 8192], BF16, off["VT"]),
                 at("w13b", [128, 8192], BF16, off["VT"] + 16384)]
        hsb = [at("ha", [128, 4 * S], BF16, off["attnF"]),
               at("hb", [128, 4 * S], BF16, off["attnF"] + 4 * S * 2)]
        probsP = [at("probsp0", [128, 2 * NQ], BF16, off["wqs0"]),
                  at("probsp1", [128, 2 * NQ], BF16, off["wqs0"] + 2 * NQ * 2)]
        NSL = (2 * S * 2 - 4 * NQ * 2) // 1024
        sb["silu_sb"] = at("silu2", [128, NSL * 512], BF16, off["wqs0"] + 4 * NQ * 2)
        w2sb = [at("w2a", [128, 4096], BF16, off["wqs0"] + 2 * S * 2),
                at("w2b", [128, 4096], BF16, off["wqs1"])]
        xrB = at("xrB", [128, 4096], BF16, off["wqs1"] + 8192)
        outr = at("outr", [128, 2048], F32, off["wqs1"] + 8192)
        if 16 * S >= 16384 + 8 * S:
            scr = off["XF"] + 16384
        else:
            scr, _ = nc.bump_sbuf(8 * S)
            scr -= base0
        sb["Ost"] = at("Ost", [128, S], BF16, scr)
        sb["rb"] = at("rb2", [128, S], BF16, scr + 2 * S)
        sb["sums"] = at("sums", [1, 2 * S], BF16, scr + 4 * S)
        xrA = at("xrA", [128, 8192], BF16, off["QF"])

        nc.psum_base = 6
        with nc.psum_tensor("tp0", [128, 1024], BF16) as tp0, \
             nc.psum_tensor("tp1", [128, 1024], BF16) as tp1:
            tp = [tp0, tp1]
        nc.psum_base = 0
        with nc.psum_tensor("qk0", [128, S], F32) as qk0, \
             nc.psum_tensor("qk1", [128, S], F32) as qk1:
            qk = [qk0, qk1]
        with nc.psum_tensor("vp0", [128, 1024], F32) as vp0, \
             nc.psum_tensor("vp1", [128, 1024], F32) as vp1:
            vp = [vp0, vp1]
        with nc.psum_tensor("SQ0", [128, 2 * NQ], F32) as SQ0, \
             nc.psum_tensor("SQ1", [128, 2 * NQ], F32) as SQ1, \
             nc.psum_tensor("OA0", [65, NQ], F32) as OA0, \
             nc.psum_tensor("OB0", [65, NQ], F32) as OB0, \
             nc.psum_tensor("OA1", [65, NQ], F32) as OA1, \
             nc.psum_tensor("OB1", [65, NQ], F32) as OB1:
            Sq = [SQ0, SQ1]
            O2 = [[OA0, OB0], [OA1, OB1]]
        with nc.psum_tensor("yp0", [128, 1024], F32) as yp0, \
             nc.psum_tensor("yp1", [128, 1024], F32) as yp1, \
             nc.psum_tensor("tq0", [128, 1024], BF16) as tq0, \
             nc.psum_tensor("tq1", [128, 1024], BF16) as tq1, \
             nc.psum_tensor("gpq", [128, 16 * NE], F32) as gpq:
            yp = [yp0, yp1]
            tq = [tq0, tq1]
        with nc.psum_tensor("hg0", [128, 512], F32) as hg0, \
             nc.psum_tensor("hu0", [128, 512], F32) as hu0, \
             nc.psum_tensor("hg1", [128, 512], F32) as hg1, \
             nc.psum_tensor("hu1", [128, 512], F32) as hu1, \
             nc.psum_tensor("y20", [128, 1024], F32) as y20, \
             nc.psum_tensor("y21", [128, 1024], F32) as y21:
            hg = [hg0, hg1]
            hu = [hu0, hu1]
            y2 = [y20, y21]
        nc.psum_base = 0
        with nc.psum_tensor("y22", [128, 1024], F32) as y22, \
             nc.psum_tensor("y23", [128, 1024], F32) as y23:
            y2x = [y22, y23]
        nc.psum_base = 0

        sems = {}
        for name in p.cnt:
            sems[name] = EC(nc.semaphore(name))


        p.op("gpsimd", lambda g: g.memset(sb["ones"][:], 1.0), inc="g_ms")
        p.op("gpsimd", lambda g: g.memset(sb["eps"][:], LN_EPS), inc="g_ms")
        p.op("gpsimd", lambda g: g.memset(
            bass.AP(VT, 64, [[T * H * 65, 128], [65, T * H]]), 1.0), inc="g_ms")
        p.op("gpsimd", lambda g: g.dma_start(sb["bv_sb"][:], bv[:, :]), inc="g_cst", n=16)
        p.op("gpsimd", lambda g: g.dma_start(sb["bout_sb"][:], bout[:, :]), inc="g_cst", n=16)
        p.op("gpsimd", lambda g: g.dma_start(
            sb["gw_sb"][:], bass.AP(gw, 0, [[NE, 128], [128 * NE, 8], [1, NE]])),
            inc="g_cst", n=16)

        xpieces = [(0, 1, "g_xf"), (1, P4 - 1, "g_x0")] + [
            (pc * P4, P4, ("g_x0" if pc % 2 == 0 else "g_x1")) for pc in range(1, 4)]
        xgate = {}
        cnts = {"g_x0": 0, "g_x1": 0, "g_xf": 0}
        for t0, ntile, semn in xpieces:
            xin = bass.AP(x, t0 * 128 * E, [[E, 128], [128 * E, ntile], [1, E]])
            slot = (t0 % 8)
            xout = xrA[:, slot * 1024: slot * 1024 + ntile * 1024]
            p.wait("gpsimd", "t_tp", t0 - P4)
            p.op("gpsimd", lambda g, a=xout, b=xin: g.dma_start(a, b), inc=semn, n=16)
            cnts[semn] += 16
            for t in range(t0, t0 + ntile):
                xgate[t] = (semn, cnts[semn])

        def wq_dma(ocbase, slot):
            src = bass.AP(wqkv, ocbase * 128,
                          [[3 * E, 128], [128 * 3 * E, 8], [1, 1024]])
            return lambda g, a=wqs[slot][:, :], b=src: g.dma_start(a, b)
        p.op("gpsimd", wq_dma(0, 0), inc="g_wq0", n=16)
        p.op("gpsimd", wq_dma(8, 1), inc="g_wq1", n=16)
        p.wait("gpsimd", "t_qk", 8)
        p.op("gpsimd", wq_dma(16, 0), inc="g_wq0", n=16)

        p.op("sync", lambda s: s.dma_start(sb["bqk_sb"][:], bqk[:, :]), inc="s_cst", n=16)
        p.op("sync", lambda s: s.dma_start(sb["gbb_sb"][:], gbb[:, :]), inc="s_cst", n=16)
        p.op("sync", lambda s: s.dma_start(sb["ident_sb"][:], ident[:, :]), inc="s_cst", n=16)
        P8 = T // 8
        for pc in range(8):
            t0 = pc * P8
            xin = bass.AP(x, t0 * 128 * E, [[E, 128], [128 * E, P8], [1, E]])
            dst = RES[:, t0 * 1024:(t0 + P8) * 1024]
            semn = X0SEMS[pc]
            p.wait("sync", "t_s", (8 * (pc % 4) + 8) * T)
            p.op("sync", lambda s, a=dst, b=xin: s.dma_start(a, b), inc=semn, n=16)

        for t in range(T):
            p.wait("tensor", "s_cst", 48)
            p.wait("tensor", xgate[t][0], xgate[t][1])
            p.wait("tensor", "v_tp", t - 1)
            slot = t % 8
            for c in range(8):
                p.op("tensor",
                     lambda te, a=tp[t % 2][:, c * 128:(c + 1) * 128],
                            b=xrA[:, slot * 1024 + c * 128: slot * 1024 + (c + 1) * 128],
                            i=sb["ident_sb"][:, :]:
                        te.matmul(a, b, i, is_transpose=True, start=True, stop=True),
                     inc=("t_tp" if c == 7 else None))
        for t in range(T):
            p.wait("vector", "t_tp", t + 1)
            dst = bass.AP(XF, t * 128, [[8 * S, 128], [S, 8], [1, 128]])
            p.op("vector", lambda v, a=dst, b=tp[t % 2][:, 0:1024]: v.tensor_copy(a, b),
                 inc="v_tp")

        for oc in range(16):
            piece = oc // 8
            if oc == 0:
                p.wait("tensor", "g_wq0", 16)
            if oc == 8:
                p.wait("tensor", "g_wq1", 16)
            p.wait("tensor", "a_qk", oc - 1)
            for g in range(S // 512):
                p.wait("tensor", "v_tp", 4 * (g + 1))
                if oc % 2 == 1 and g >= 2:
                    p.wait("tensor", "v_tp", T - 3 + g)
                for kc in range(8):
                    lhs = wqs[piece][:, kc * 1024 + (oc % 8) * 128: kc * 1024 + (oc % 8) * 128 + 128]
                    p.op("tensor",
                         lambda te, a=qk[oc % 2][:, g * 512:(g + 1) * 512], l=lhs,
                                r=XF[:, kc * S + g * 512: kc * S + (g + 1) * 512],
                                st=(kc == 0), sp=(kc == 7):
                            te.matmul(a, l, r, start=st, stop=sp),
                         inc=("t_qk" if (kc == 7 and g == S // 512 - 1) else None))
        for oc in range(16):
            p.wait("scalar", "t_qk", oc + 1)
            p.wait("scalar", "s_cst", 48)
            dstT = QF if oc < 8 else KF
            dst = dstT[:, (oc % 8) * S:(oc % 8 + 1) * S]
            p.op("scalar",
                 lambda sc, a=dst, b=qk[oc % 2][:, 0:S], bias=sb["bqk_sb"][:, oc:oc + 1]:
                    sc.activation(a, b, AF.Identity, bias=bias, scale=1.0),
                 inc="a_qk")
        for tt in range(T):
            if tt == 0:
                p.wait("tensor", "a_qk", 16)
                p.wait("tensor", "g_wq0", 32)
                p.wait("tensor", "g_ms", 3)
                p.wait("tensor", "g_cst", 48)
            p.wait("tensor", "v_v", tt - 1)
            for kc in range(8):
                lhs = XF[:, kc * S + tt * 128: kc * S + (tt + 1) * 128]
                for g in range(2):
                    rhs = wqs[0][:, kc * 1024 + g * 512: kc * 1024 + (g + 1) * 512]
                    p.op("tensor",
                         lambda te, a=vp[tt % 2][:, g * 512:(g + 1) * 512], l=lhs, r=rhs,
                                st=(kc == 0):
                            te.matmul(a, l, r, start=st, stop=False))
            for g in range(2):
                p.op("tensor",
                     lambda te, a=vp[tt % 2][:, g * 512:(g + 1) * 512],
                            l=sb["ones"][0:1, 0:128], r=sb["bv_sb"][0:1, g * 512:(g + 1) * 512]:
                        te.matmul(a, l, r, start=False, stop=True),
                     inc=("t_v" if g == 1 else None))
        for tt in range(T):
            p.wait("vector", "t_v", tt + 1)
            dst = bass.AP(VT, tt * H * 65, [[T * H * 65, 128], [65, H], [1, 64]])
            p.op("vector", lambda v, a=dst, b=vp[tt % 2][:, 0:1024]: v.tensor_copy(a, b),
                 inc="v_v")

        def w2_dma(e):
            src = bass.AP(w2, e * INTER * E,
                          [[E, 128], [128 * E, 4], [1, E]])
            return lambda g, a=w2sb[e % 2], b=src: g.dma_start(a[:, :], b)
        p.wait("gpsimd", "t_qk", 16)
        p.op("gpsimd", w2_dma(1), inc="g_w2b", n=16)
        p.wait("gpsimd", "t_v", T)
        p.op("gpsimd", lambda g: g.dma_start(
            woutsb[:, :], bass.AP(wout, 0, [[E, 128], [128 * E, 8], [1, E]])),
            inc="g_wo", n=16)
        p.op("gpsimd", w2_dma(0), inc="g_w2a", n=16)

        NG = 32 * T

        def attn_evac(sqa):
            hpa, qqa = sqa // 4, sqa % 4
            p.wait("vector", "t_av", (sqa + 1) * T)
            if sqa >= 4:
                p.wait("vector", "v_div", sqa - 3)
                p.wait("vector", "g_rb", 32 * (sqa - 3))
            for hx in range(2):
                po = hx * 64
                p.op("vector", lambda v, a=sb["Ost"][po:po + 64, qqa * NQ:(qqa + 1) * NQ],
                            b=O2[sqa % 2][hx][0:64, 0:NQ]: v.tensor_copy(a, b),
                     inc="v_os", lp="softmax denominator rounded to bf16")
                p.op("vector", lambda v, a=sb["sums"][0:1, hx * S + qqa * NQ: hx * S + (qqa + 1) * NQ],
                            b=O2[sqa % 2][hx][64:65, 0:NQ]: v.tensor_copy(a, b),
                     inc="v_os", lp="softmax denominator rounded to bf16")
            p.wait("gpsimd", "v_os", 4 * (sqa + 1))
            if sqa >= 4:
                p.wait("gpsimd", "v_div", sqa - 3)
            p.op("gpsimd", lambda g, a=sb["rb"][0:64, qqa * NQ:(qqa + 1) * NQ],
                        b=bass.AP(sb["sums"], qqa * NQ, [[2 * S, 1], [0, 64], [1, NQ]]):
                 g.dma_start(a, b), inc="g_rb", n=16)
            p.op("gpsimd", lambda g, a=sb["rb"][64:128, qqa * NQ:(qqa + 1) * NQ],
                        b=bass.AP(sb["sums"], S + qqa * NQ, [[2 * S, 1], [0, 64], [1, NQ]]):
                 g.dma_start(a, b), inc="g_rb", n=16)
            p.wait("vector", "g_rb", 32 * (sqa + 1))
            p.op("vector", lambda v, a=sb["rb"][:, qqa * NQ:(qqa + 1) * NQ]:
                 v.reciprocal(a, a), inc="v_rc", lp="softmax denominator rounded to bf16")
            p.wait("vector", "v_rc", sqa + 1)
            p.op("vector",
                 lambda v, a=attnF[:, hpa * S + qqa * NQ: hpa * S + (qqa + 1) * NQ],
                        b=sb["Ost"][:, qqa * NQ:(qqa + 1) * NQ],
                        c=sb["rb"][:, qqa * NQ:(qqa + 1) * NQ]:
                    v.tensor_tensor(a, b, c, op=OP.mult),
                 inc="v_div")

        for gk in range(NG + 2):
            if gk >= 2:
                av = gk - 2
                sqa, ktv = av // T, av % T
                hpa = sqa // 4
                heads = (2 * hpa, 2 * hpa + 1)
                if ktv == 0 and sqa >= 2:
                    p.wait("tensor", "v_os", 4 * (sqa - 1))
                p.wait("tensor", "a_e", av + 1)
                for hx in range(2):
                    p.op("tensor",
                         lambda te, a=O2[sqa % 2][hx][0:65, 0:NQ],
                                l=VT[:, ktv * H * 65 + heads[hx] * 65: ktv * H * 65 + heads[hx] * 65 + 65],
                                r=probsP[ktv % 2][:, hx * NQ:(hx + 1) * NQ],
                                st=(ktv == 0), sp=(ktv == T - 1):
                            te.matmul(a, l, r, start=st, stop=sp),
                         inc=("t_av" if hx == 1 else None))
                if ktv == T - 1:
                    attn_evac(sqa)
            if gk < NG:
                sqi, kt = gk // T, gk % T
                hp, qq = sqi // 4, sqi % 4
                if gk == 0:
                    p.wait("tensor", "v_v", T)
                if gk >= 2:
                    p.wait("tensor", "a_e", gk - 1)
                for hx in range(2):
                    po = hx * 64
                    p.op("tensor",
                         lambda te, a=Sq[kt % 2][:, hx * NQ:(hx + 1) * NQ],
                                l=KF[po:po + 64, hp * S + kt * 128: hp * S + (kt + 1) * 128],
                                r=QF[po:po + 64, hp * S + qq * NQ: hp * S + (qq + 1) * NQ]:
                            te.matmul(a, l, r, start=True, stop=True),
                         inc=("t_s" if hx == 1 else None))
                if gk == 0:
                    p.wait("scalar", "t_v", T)
                p.wait("scalar", "t_s", gk + 1)
                p.op("scalar",
                     lambda sc, a=probsP[kt % 2][:, 0:2 * NQ], b=Sq[kt % 2][:, 0:2 * NQ]:
                        sc.activation(a, b, AF.Exp, scale=0.125),
                     inc="a_e")

        def w13_dma(e):
            src = bass.AP(w13, e * E * 2 * INTER,
                          [[2 * INTER, 128], [128 * 2 * INTER, 8], [1, 1024]])
            return lambda g, a=w13sb[e % 2][:, :], b=src: g.dma_start(a, b)
        p.wait("gpsimd", "t_av", 32 * T)
        p.op("gpsimd", w13_dma(0), inc="g_w13a", n=16)
        p.op("gpsimd", w13_dma(1), inc="g_w13b", n=16)

        for e in range(2, NX if not stop_after else 0):
            p.wait("gpsimd", "t_h13", (e - 1) * 4 * TG)
            p.op("gpsimd", w13_dma(e), inc=("g_w13a" if e % 2 == 0 else "g_w13b"), n=16)
            p.wait("gpsimd", "t_w2", (e - 1) * T)
            p.setfloor("gpsimd", "t_h13", (e - 1) * 4 * TG)
            p.op("gpsimd", w2_dma(e), inc=("g_w2a" if e % 2 == 0 else "g_w2b"), n=16)

        ln_rdy = {}

        def applycast(tt):
            rtc = RES[:, tt * 1024:(tt + 1) * 1024]
            p.wait("scalar", "v_ln", ln_rdy[tt])
            p.wait("scalar", "t_tp2", tt - 3)
            p.op("scalar", lambda sc, a=xrB[:, (tt % 4) * 1024:((tt % 4) + 1) * 1024],
                        b=rtc, t=tt: sc.activation(a, b, AF.Identity,
                        bias=sb["nm"][:, t:t + 1], scale=sb["rs"][:, t:t + 1]),
                 inc="a_xc")
        for tt in range(T):
            if tt == 0:
                p.wait("tensor", "g_wo", 16)
                p.wait("tensor", "a_e", p.cnt["a_e"])
                if NQ < 512:
                    p.wait("tensor", "v_os", p.cnt["v_os"])
                for s_ in ("t_s", "t_av", "v_os", "v_rc"):
                    p.setfloor("tensor", s_, p.cnt[s_])
            p.wait("tensor", "v_res", tt - 1)
            for kc in range(8):
                p.wait("tensor", "v_div", 4 * kc + tt // 4 + 1)
                lhs = attnF[:, kc * S + tt * 128: kc * S + (tt + 1) * 128]
                for j in range(2):
                    p.op("tensor",
                         lambda te, a=yp[tt % 2][:, j * 512:(j + 1) * 512], l=lhs,
                                r=woutsb[:, kc * 1024 + j * 512: kc * 1024 + (j + 1) * 512],
                                st=(kc == 0):
                            te.matmul(a, l, r, start=st, stop=False))
            for j in range(2):
                p.op("tensor",
                     lambda te, a=yp[tt % 2][:, j * 512:(j + 1) * 512],
                            l=sb["ones"][0:1, 0:128], r=sb["bout_sb"][0:1, j * 512:(j + 1) * 512]:
                        te.matmul(a, l, r, start=False, stop=True),
                     inc=("t_y" if j == 1 else None))
        for tt in range(T):
            p.wait("vector", "t_y", tt + 1)
            p.wait("vector", X0SEMS[tt // (T // 8)], 16)
            rt = RES[:, tt * 1024:(tt + 1) * 1024]
            p.op("vector", lambda v, a=rt, b=yp[tt % 2][:, 0:1024]: v.tensor_tensor(a, b, a, op=OP.add),
                 inc="v_res")
            p.wait("vector", "v_res", tt + 1)
            p.wait("vector", "v_ln", p.cnt["v_ln"])
            p.op("vector", lambda v, a=rt: v.bn_stats(sb["stats"][:, 0:6], a[:, 0:512]), inc="v_ln")
            p.op("vector", lambda v, a=rt: v.bn_stats(sb["stats"][:, 6:12], a[:, 512:1024]), inc="v_ln")
            p.wait("vector", "v_ln", p.cnt["v_ln"])
            p.op("vector", lambda v, t=tt: v.bn_aggr(sb["mv"][:, 2 * t:2 * t + 2], sb["stats"][:, 0:12]),
                 inc="v_ln")
            p.wait("scalar", "v_ln", p.cnt["v_ln"])
            p.wait("scalar", "g_ms", 2)
            p.op("scalar", lambda sc, t=tt: sc.activation(
                sb["sq"][:, t:t + 1], sb["mv"][:, 2 * t + 1:2 * t + 2], AF.Sqrt,
                bias=sb["eps"][:, 0:1]), inc="a_sq")
            p.wait("vector", "a_sq", tt + 1)
            p.op("vector", lambda v, t=tt: v.reciprocal(sb["rs"][:, t:t + 1], sb["sq"][:, t:t + 1]),
                 inc="v_ln")
            p.wait("vector", "v_ln", p.cnt["v_ln"])
            p.op("vector", lambda v, t=tt: v.tensor_scalar(
                sb["nm"][:, t:t + 1], sb["mv"][:, 2 * t:2 * t + 1], sb["rs"][:, t:t + 1],
                -1.0, op0=OP.mult, op1=OP.mult), inc="v_ln")
            ln_rdy[tt] = p.cnt["v_ln"]
            if tt == T // 2 - 1:
                for k_ in range(min(4, T // 2)):
                    applycast(k_)
        for tt in range(T):
            rt = RES[:, tt * 1024:(tt + 1) * 1024]
            p.wait("vector", "a_xc", tt + 1)
            p.op("vector", lambda v, a=rt, t=tt: v.tensor_scalar(
                a, a, sb["mv"][:, 2 * t:2 * t + 1], sb["rs"][:, t:t + 1],
                op0=OP.subtract, op1=OP.mult), inc="v_ap")
        def xf_evac(k):
            p.wait("scalar", "t_tp2", k + 1)
            dst = bass.AP(XF, k * 128, [[8 * S, 128], [S, 8], [1, 128]])
            p.op("scalar", lambda sc, a=dst, b=tq[k % 2][:, 0:1024]: sc.copy(a, b),
                 inc="v_tp2")

        for tt in range(min(4, T // 2), T):
            applycast(tt)
            if tt >= 4:
                xf_evac(tt - 4)
        for k_ in range(T - 4, T):
            xf_evac(k_)
        if not stop_after:
            QT = T // 4
            for q in range(4):
                p.wait("scalar", "t_g", QT * (q + 1))
                p.op("scalar", lambda sc, c0=q * QT * NE, c1=(q + 1) * QT * NE:
                     sc.activation(sb["scoresg"][:, c0:c1], gpq[:, c0:c1], AF.Sigmoid),
                     inc="a_g")
        def gate_mm(gt):
            p.wait("tensor", "v_tp2", gt + 1)
            for kc in range(8):
                p.op("tensor",
                     lambda te, a=gpq[:, gt * NE:(gt + 1) * NE],
                            l=XF[:, kc * S + gt * 128: kc * S + (gt + 1) * 128],
                            r=sb["gw_sb"][:, kc * NE:(kc + 1) * NE],
                            st=(kc == 0), sp=(kc == 7):
                        te.matmul(a, l, r, start=st, stop=sp),
                     inc=("t_g" if kc == 7 else None))

        for tt in range(T):
            p.wait("tensor", "a_xc", tt + 1)
            p.wait("tensor", "v_tp2", tt - 1)
            for c in range(8):
                p.op("tensor",
                     lambda te, a=tq[tt % 2][:, c * 128:(c + 1) * 128],
                            b=xrB[:, (tt % 4) * 1024 + c * 128:(tt % 4) * 1024 + (c + 1) * 128],
                            i=sb["ident_sb"][:, :]:
                        te.matmul(a, b, i, is_transpose=True, start=True, stop=True),
                     inc=("t_tp2" if c == 7 else None))
            if not stop_after and tt >= 3:
                gate_mm(tt - 3)
        if not stop_after:
            for gt in range(T - 3, T):
                gate_mm(gt)



        vtk_after = 0

        def topk_all():
            p.wait("vector", "a_g", 4)
            p.wait("vector", "s_cst", 48)
            SG = sb["scoresg"]
            W, M, m = sb["tkW"], sb["tkM"], sb["tkm"]
            seg = lambda t_: bass.AP(t_, 0, [[NE * T, 128], [NE, T], [1, NE]])
            mb = bass.AP(m, 0, [[T, 128], [1, T], [0, NE]])
            gbbb = bass.AP(sb["gbb_sb"], 0, [[NE, 128], [0, T], [1, NE]])
            B = sb["combT"]

            def vtk(fn):
                p.op("vector", fn, inc="v_tk")
                p.wait("vector", "v_tk", p.cnt["v_tk"])

            vtk(lambda v: v.tensor_tensor(B[:], SG[:], gbbb, op=OP.add))
            vtk(lambda v: v.tensor_copy(W[:], B[:]))
            for it in range(TOPK - 1):
                vtk(lambda v: v.reduce_max(m[:, 0:T], seg(W), axis=AX.X))
                vtk(lambda v: v.tensor_tensor(M[:], W[:], mb, op=OP.is_ge))
                vtk(lambda v: v.scalar_tensor_tensor(W[:], M[:], -BIG,
                                                     W[:], op0=OP.mult, op1=OP.add))
            vtk(lambda v: v.reduce_max(m[:, 0:T], seg(W), axis=AX.X))
            vtk(lambda v: v.tensor_tensor(M[:], B[:], mb, op=OP.is_ge))
            vtk(lambda v: v.tensor_tensor(W[:], SG[:], M[:], op=OP.mult))
            vtk(lambda v: v.reduce_sum(m[:, 0:T], seg(W), axis=AX.X))
            vtk(lambda v: v.reciprocal(m[:, 0:T], m[:, 0:T]))
            vtk(lambda v: v.scalar_tensor_tensor(B[:], W[:], ROUTE_SCALE,
                                                 mb, op0=OP.mult, op1=OP.mult))

        def ln2_tile(tt):
            p.wait("vector", "v_y", NE * T + tt + 1)
            rt = RES[:, tt * 1024:(tt + 1) * 1024]
            if tt == 0:
                p.wait("vector", "v_ln", 4 * T)
            p.wait("vector", "v_ln2", p.cnt["v_ln2"])
            p.op("vector", lambda v, a=rt: v.bn_stats(sb["stats"][:, 0:6], a[:, 0:512]), inc="v_ln2")
            p.op("vector", lambda v, a=rt: v.bn_stats(sb["stats"][:, 6:12], a[:, 512:1024]), inc="v_ln2")
            p.wait("vector", "v_ln2", p.cnt["v_ln2"])
            p.op("vector", lambda v, t=tt: v.bn_aggr(sb["mv2"][:, 2 * t:2 * t + 2], sb["stats"][:, 0:12]),
                 inc="v_ln2")
            p.wait("scalar", "v_ln2", p.cnt["v_ln2"])
            p.op("scalar", lambda sc, t=tt: sc.activation(
                sb["sq2"][:, t:t + 1], sb["mv2"][:, 2 * t + 1:2 * t + 2], AF.Sqrt,
                bias=sb["eps"][:, 0:1]), inc="a_sq2")
            p.wait("vector", "a_sq2", tt + 1)
            p.op("vector", lambda v, t=tt: v.reciprocal(sb["rs2"][:, t:t + 1], sb["sq2"][:, t:t + 1]),
                 inc="v_ln2")
            p.wait("vector", "v_ln2", p.cnt["v_ln2"])
            p.wait("vector", ("s_y0" if tt % 2 == 0 else "s_y1"), 16 * (tt // 2))
            p.op("vector", lambda v, a=outr[:, (tt % 2) * 1024:((tt % 2) + 1) * 1024],
                        b=rt, t=tt: v.tensor_scalar(
                a, b, sb["mv2"][:, 2 * t:2 * t + 1], sb["rs2"][:, t:t + 1],
                op0=OP.subtract, op1=OP.mult), inc="v_o")
            p.wait("sync", "v_o", tt + 1)
            p.op("sync", lambda s, t=tt: s.dma_start(
                y[t * 128:(t + 1) * 128, :], outr[:, (t % 2) * 1024:((t % 2) + 1) * 1024]),
                inc=("s_y0" if tt % 2 == 0 else "s_y1"), n=16)

        if not stop_after:
            topk_all()
            vtk_after = p.cnt["v_tk"]
        for e in range(NX if not stop_after else 0):
            for tg in range(TG):
                for pp in range(4):
                    u = e * 4 * TG + tg * 4 + pp
                    if pp == 0 and tg == 0:
                        p.wait("tensor", ("g_w13a" if e % 2 == 0 else "g_w13b"),
                               16 * (e // 2 + 1))
                        if e == 0:
                            p.wait("tensor", "v_res", T)
                    if e == 0:
                        p.wait("tensor", "v_tp2", 4 * tg + 4)
                    p.wait("tensor", "v_h", u - 1)
                    for kc in range(8):
                        lg = w13sb[e % 2][:, kc * 1024 + pp * 128: kc * 1024 + (pp + 1) * 128]
                        lu = w13sb[e % 2][:, kc * 1024 + (pp + 4) * 128: kc * 1024 + (pp + 5) * 128]
                        rhs = XF[:, kc * S + tg * 512: kc * S + (tg + 1) * 512]
                        p.op("tensor", lambda te, a=hg[pp % 2][:, 0:512], l=lg, r=rhs,
                                    st=(kc == 0), sp=(kc == 7):
                             te.matmul(a, l, r, start=st, stop=sp))
                        p.op("tensor", lambda te, a=hu[pp % 2][:, 0:512], l=lu, r=rhs,
                                    st=(kc == 0), sp=(kc == 7):
                             te.matmul(a, l, r, start=st, stop=sp),
                             inc=("t_h13" if kc == 7 else None))
            for tg in range(TG):
                for pp in range(4):
                    u = e * 4 * TG + tg * 4 + pp
                    p.wait("scalar", "t_h13", u + 1)
                    p.wait("scalar", "v_h", u - (NSL - 1))
                    p.op("scalar", lambda sc, a=sb["silu_sb"][:, (u % NSL) * 512:((u % NSL) + 1) * 512],
                                b=hg[pp % 2][:, 0:512]: sc.activation(a, b, AF.Silu),
                         inc="a_si")
                    p.wait("vector", "a_si", u + 1)
                    if pp == 0 and tg == 0:
                        p.wait("vector", "t_w2", (e - 1) * T)
                    p.op("vector", lambda v, a=hsb[e % 2][:, pp * S + tg * 512: pp * S + (tg + 1) * 512],
                                b=sb["silu_sb"][:, (u % NSL) * 512:((u % NSL) + 1) * 512],
                                c=hu[pp % 2][:, 0:512]:
                            v.tensor_tensor(a, b, c, op=OP.mult),
                         inc="v_h")

            for tt in range(T):
                wide = (e == NX - 1 and tt >= T - 4 and tt < T - 2)
                yb = y2x[tt % 2] if wide else y2[tt % 2]
                if tt == 0:
                    p.wait("tensor", ("g_w2a" if e % 2 == 0 else "g_w2b"), 16 * (e // 2 + 1))
                p.wait("tensor", "v_h", e * 4 * TG + (tt // 4) * 4 + 4)
                if not wide:
                    p.wait("tensor", "v_y", e * T + tt - 1)
                if e == 0 and tt <= 1:
                    if tt == 0:
                        p.setfloor("tensor", "v_tp2", T)
                    if tt == 1:
                        p.wait("tensor", "a_g", 4)
                for oc in range(4):
                    lhs = hsb[e % 2][:, oc * S + tt * 128: oc * S + (tt + 1) * 128]
                    for j in range(2):
                        p.op("tensor", lambda te, a=yb[:, j * 512:(j + 1) * 512], l=lhs,
                                    r=w2sb[e % 2][:, oc * 1024 + j * 512: oc * 1024 + (j + 1) * 512],
                                    st=(oc == 0), sp=(oc == 3):
                                te.matmul(a, l, r, start=st, stop=sp),
                             inc=("t_w2" if (oc == 3 and j == 1) else None))
            for tt in range(T):
                p.wait("vector", "t_w2", e * T + tt + 1)
                if e == 0:
                    p.wait("vector", "v_tk", vtk_after)
                if e == 0:
                    p.wait("vector", "a_xc", tt + 1)
                p.wait("vector", "v_y", e * T + tt)
                rt = RES[:, tt * 1024:(tt + 1) * 1024]
                widec = (e == NX - 1 and tt >= T - 4 and tt < T - 2)
                ybc = y2x[tt % 2] if widec else y2[tt % 2]
                if e < NE:
                    p.op("vector", lambda v, a=rt, t=tt, ee=e: v.scalar_tensor_tensor(
                        a, y2[t % 2][:, 0:1024], sb["combT"][:, t * NE + ee: t * NE + ee + 1], a,
                        op0=OP.mult, op1=OP.add), inc="v_y")
                else:
                    p.op("vector", lambda v, a=rt, b=ybc: v.tensor_tensor(
                        a, b[:, 0:1024], a, op=OP.add), inc="v_y")
                    ln2_tile(tt)

        if not stop_after:
            p.wait("sync", "s_y0", 16 * (T // 2))
            p.wait("sync", "s_y1", 16 * (T // 2))
        else:
            p.wait("sync", "v_tp2", T)

        dbg_map = {"res": RES, "comb": sb["combT"], "scoresg": sb["scoresg"],
                   "xf": XF, "attnf": attnF, "qf": QF, "kf": KF, "vt": VT}
        for name in debug:
            h_ = dbg_map[name]
            dt_np = F32 if h_.dtype == F32 else BF16
            d = nc.dram_tensor("dbg_" + name, list(h_.shape), dt_np, kind="ExternalOutput")
            dbg_tensors[name] = "dbg_" + name
            p.op("sync", lambda s, a=d, b=h_: s.dma_start(a[:, :], b[:]), inc="s_dbg", n=16)
        if debug:
            p.wait("sync", "s_dbg", 16 * len(debug))

        block = EC(nc.Block())

        def replay(engname):
            def body(eng):
                for item in p.q[engname]:
                    if item[0] == "w":
                        eng.wait_ge(sems[item[1]], item[2])
                    else:
                        _, fn, inc, n, lp = item
                        if lp:
                            with nc.allow_low_precision(reason=lp):
                                ins = fn(eng)
                        else:
                            ins = fn(eng)
                        if inc is not None:
                            ins.then_inc(sems[inc], n)
            return body

        block.sync(replay("sync"))
        block.gpsimd(replay("gpsimd"))
        block.tensor(replay("tensor"))
        block.vector(replay("vector"))
        block.scalar(replay("scalar"))

    return nc, dbg_tensors



def _prep_weights(Wqkv, bqkv, Wout, bout_, gate_w, gate_b, w13_, w2_, w13s, w2s):
    E_, H_, HD_ = 1024, 16, 64
    qc = np.concatenate([np.arange(h * 192, h * 192 + 64) for h in range(H_)])
    kc = qc + 64
    vc = qc + 128
    perm = np.concatenate([qc, kc, vc])
    wqkv_r = np.ascontiguousarray(Wqkv[:, perm], dtype=np.float32)
    bqkv_r = np.ascontiguousarray(bqkv[perm], dtype=np.float32)
    bqk = np.ascontiguousarray(bqkv_r[:2048].reshape(16, 128).T, dtype=np.float32)
    bv_ = np.ascontiguousarray(bqkv_r[2048:].reshape(1, 1024), dtype=np.float32)
    w13T = np.ascontiguousarray(
        np.concatenate([np.transpose(w13_, (0, 2, 1)),
                        w13s.T[None, :, :]], axis=0), dtype=np.float32)
    w2T = np.ascontiguousarray(
        np.concatenate([np.transpose(w2_, (0, 2, 1)),
                        w2s.T[None, :, :]], axis=0), dtype=np.float32)
    return {
        "wqkv": wqkv_r,
        "bqk": bqk,
        "bv": bv_,
        "wout": np.ascontiguousarray(Wout, dtype=np.float32),
        "bout": np.ascontiguousarray(np.asarray(bout_).reshape(1, 1024), dtype=np.float32),
        "gw": np.ascontiguousarray(gate_w.T, dtype=np.float32),
        "gbb": np.ascontiguousarray(np.tile(np.asarray(gate_b).reshape(1, 16), (128, 1)),
                                    dtype=np.float32),
        "w13": w13T,
        "w2": w2T,
        "ident": np.eye(128, dtype=ml_dtypes.bfloat16),
    }


def kernel(L0_attn_x, L0_attn_Wqkv, L0_attn_bqkv, L0_attn_Wout, L0_attn_bout,
           L0_ffn_input_ids, L0_ffn_gate_w, L0_ffn_gate_b, L0_ffn_w13,
           L0_ffn_w2, L0_ffn_w13_s, L0_ffn_w2_s):
    xx = np.ascontiguousarray(np.asarray(L0_attn_x), dtype=np.float32)
    B = xx.shape[0]
    weights = _prep_weights(
        np.asarray(L0_attn_Wqkv), np.asarray(L0_attn_bqkv),
        np.asarray(L0_attn_Wout), np.asarray(L0_attn_bout),
        np.asarray(L0_ffn_gate_w), np.asarray(L0_ffn_gate_b),
        np.asarray(L0_ffn_w13), np.asarray(L0_ffn_w2),
        np.asarray(L0_ffn_w13_s), np.asarray(L0_ffn_w2_s))

    nc, _dbg = build(S=2048)
    in_maps = []
    for c in range(B):
        m = dict(weights)
        m["x"] = np.ascontiguousarray(xx[c])
        in_maps.append(m)
    res = run_bass_kernel_spmd(nc, in_maps, core_ids=list(range(B)))
    out = np.stack([np.asarray(res.results[c]["y"]) for c in range(B)], axis=0)
    return out.astype(np.float32)
